# version 01
# baseline (speedup 1.0000x reference)
import math
import numpy as np
import ml_dtypes

import concourse.bass as bass
import concourse.mybir as mybir
from concourse.bass_utils import run_bass_kernel_spmd

F32 = mybir.dt.float32
BF16 = mybir.dt.bfloat16
I32 = mybir.dt.int32
I8 = mybir.dt.int8
Alu = mybir.AluOpType
Act = mybir.ActivationFunctionType
AX = mybir.AxisListType

D = 2048
T = 4096
HID = 8192
NHEAD = 16
NT = 1024
CSC = 1.0 / math.sqrt(128.0)
EPS = 1e-5
MAGIC = 0x5F3759DF
EXPMASK = 0x7F800000
EXPCLAMP = float(np.float32(2.0 ** -87))

STAGE = 5
PH45_TRIM = False
PH6_STOP = 0
PH6_DBG = 0


class Cnt:
    def __init__(self, nc, name):
        self.h = nc.alloc_semaphore(name)
        self.n = 0

    def inc(self, instr, k=1):
        instr.then_inc(self.h, k)
        self.n += k
        return instr

    def incd(self, instr):
        return self.inc(instr, 16)

    def reserve(self, k):
        base = self.n
        self.n += k
        return base

    def wait(self, eng, n=None):
        v = self.n if n is None else n
        if v > 0:
            eng.wait_ge(self.h, v)


def _newton_rsqrt(vector, y, v, tmp, hz, paranoid=False):
    def w(instr):
        if paranoid:
            hz.inc(instr)
            hz.wait(vector)
        return instr
    w(vector.tensor_scalar(y.bitcast(I32), v.bitcast(I32), 1, None, Alu.logical_shift_right))
    hz.inc(vector.tensor_scalar(y.bitcast(I32), y.bitcast(I32), -1, MAGIC, Alu.mult, Alu.add))
    hz.wait(vector)
    last = None
    for it in range(3):
        w(vector.tensor_mul(tmp, y, y))
        w(vector.tensor_mul(tmp, tmp, v))
        w(vector.tensor_scalar(tmp, tmp, -0.5, 1.5, Alu.mult, Alu.add))
        last = vector.tensor_mul(y, y, tmp)
        if paranoid and it < 2:
            w(last)
    return last


def _quant_ops(vector, src2, W, amx, msk, scl, rcl, q8t, outt, hz):
    s3 = src2.rearrange("p (w b) -> p w b", b=32)
    hz.inc(vector.tensor_reduce(amx.unsqueeze(2), s3, AX.X, Alu.max,
                                apply_absolute_value=True))
    hz.wait(vector)
    hz.inc(vector.tensor_scalar(msk.bitcast(I32), amx.bitcast(I32), EXPMASK, None,
                                Alu.bitwise_and))
    hz.wait(vector)
    hz.inc(vector.tensor_scalar(scl, msk, EXPCLAMP, 0.015625, Alu.max, Alu.mult))
    hz.wait(vector)
    hz.inc(vector.reciprocal(rcl, scl))
    hz.wait(vector)
    hz.inc(vector.tensor_tensor(q8t.rearrange("p (w b) -> p w b", b=32), s3,
                                rcl.unsqueeze(2).broadcast_to([128, W, 32]), Alu.mult))
    hz.wait(vector)
    return vector.tensor_tensor(outt.rearrange("p (w b) -> p w b", b=32),
                                q8t.rearrange("p (w b) -> p w b", b=32),
                                scl.unsqueeze(2).broadcast_to([128, W, 32]), Alu.mult)


def build_nc():
    nc = bass.Bass("TRN2", target_bir_lowering=False)

    xT = nc.dram_tensor("xT", [D, T], F32, kind="ExternalInput").ap()
    xov = nc.dram_tensor("xov", [NT, D], F32, kind="ExternalInput").ap()
    wqd = nc.dram_tensor("wqd", [D, D], F32, kind="ExternalInput").ap()
    wkv = nc.dram_tensor("wkv", [D, 256], F32, kind="ExternalInput").ap()
    wod = nc.dram_tensor("wod", [D, D], F32, kind="ExternalInput").ap()
    W1 = nc.dram_tensor("W1", [HID, D], F32, kind="ExternalInput").ap()
    W2 = nc.dram_tensor("W2", [D, HID], F32, kind="ExternalInput").ap()
    cosk = nc.dram_tensor("cosk", [128, T], F32, kind="ExternalInput").ap()
    sink = nc.dram_tensor("sink", [128, T], F32, kind="ExternalInput").ap()
    idn_d = nc.dram_tensor("idn", [128, 128], BF16, kind="ExternalInput").ap()
    b1dr = nc.dram_tensor("b1dr", [1, HID], F32, kind="ExternalInput").ap()
    b2c = nc.dram_tensor("b2c", [128, 16], F32, kind="ExternalInput").ap()
    lnwx = nc.dram_tensor("lnwx", [128, D], F32, kind="ExternalInput").ap()
    lnbx = nc.dram_tensor("lnbx", [128, D], F32, kind="ExternalInput").ap()
    out_d = nc.dram_tensor("out", [NT, D], F32, kind="ExternalOutput").ap()

    W1q = nc.dram_tensor("W1q", [HID, D], BF16, kind="Internal").ap()
    W2q = nc.dram_tensor("W2q", [D, HID], BF16, kind="Internal").ap()
    W2qT = nc.dram_tensor("W2qT", [HID, D], BF16, kind="Internal").ap()
    hq_d = nc.dram_tensor("hq_d", [NT, D], BF16, kind="Internal").ap()
    gq_d = nc.dram_tensor("gq_d", [NT, HID], BF16, kind="Internal").ap()
    x2_d = nc.dram_tensor("x2_d", [NT, D], F32, kind="Internal").ap()
    xf_d = nc.dram_tensor("xf_d", [NT, D], BF16, kind="Internal").ap()

    dbg = {}
    if STAGE >= 5 and PH6_DBG:
        def dout6(name, shape, dt):
            dbg[name] = nc.dram_tensor("dbg_" + name, shape, dt, kind="ExternalOutput").ap()
        dout6("hqT6", [128, 16 * 512], BF16)
        dout6("gqT6", [128, 64 * 512], BF16)
        dout6("h2T6", [128, 16 * 512], BF16)
        dout6("gq6", [NT, HID], BF16)
        dout6("xfd6", [NT, D], BF16)
        dout6("x2d6", [NT, D], F32)
    if STAGE < 5:
        def dout(name, shape, dt):
            dbg[name] = nc.dram_tensor("dbg_" + name, shape, dt, kind="ExternalOutput").ap()
        dout("q16", [128, NHEAD * NT], BF16)
        dout("k16", [128, T], BF16)
        dout("v16", [128, T], BF16)
        dout("murow", [1, 1024], F32)
        dout("rstdrow", [1, 1024], F32)
        dout("negM", [128, 1], F32)
        dout("gq2", [1, 1], F32)
        dout("gk2", [1, 1], F32)
        dout("srow", [1, 1024], F32)
        dout("sqnl", [128, 1024], F32)
        dout("aoT", [128, NHEAD * NT], BF16)
        dout("hq", [NT, D], BF16)
        dout("x2", [NT, D], F32)
        dout("xf", [NT, D], BF16)
        dout("W1q", [HID, D], BF16)
        dout("W2q", [D, HID], BF16)
        dout("stats", [128, 16], F32)
        dout("sacca", [128, 2], F32)

    xT_v = xT.rearrange("(d p) t -> p d t", p=128)
    wq_v = wqd.rearrange("(d p) c -> p d c", p=128)
    wkv_v = wkv.rearrange("(d p) c -> p d c", p=128)
    wo_v = wod.rearrange("(d p) c -> p d c", p=128)
    W1_v = W1.rearrange("(a p) d -> p a d", p=128)
    W2_v = W2.rearrange("(a p) h -> p a h", p=128)
    W1q_v = W1q.rearrange("(a p) d -> p a d", p=128)
    W2q_v = W2q.rearrange("(a p) h -> p a h", p=128)
    W2qT_v = W2qT.rearrange("(a p) d -> p a d", p=128)
    hq_v = hq_d.rearrange("(a p) d -> p a d", p=128)
    gq_v = gq_d.rearrange("(a p) h -> p a h", p=128)
    x2_v = x2_d.rearrange("(a p) d -> p a d", p=128)
    xf_v = xf_d.rearrange("(a p) d -> p a d", p=128)
    xov_v = xov.rearrange("(a p) d -> p a d", p=128)
    out_v = out_d.rearrange("(a p) d -> p a d", p=128)

    C = lambda name: Cnt(nc, name)
    s_xld = C("s_xld"); s_wld = C("s_wld"); s_fld = C("s_fld")
    s_sq = C("s_sq"); s_smm = C("s_smm"); s_vst = C("s_vst"); s_exp = C("s_exp")
    s_qkv = C("s_qkv"); s_sc = C("s_sc"); s_qev = C("s_qev")
    s_rot = C("s_rot"); s_rope = C("s_rope"); s_vt = C("s_vt"); s_vtc = C("s_vtc")
    s_nsq = C("s_nsq"); s_nmm = C("s_nmm"); s_nrm = C("s_nrm")
    v_hz = C("v_hz")
    f_qk = C("f_qk"); f_e = C("f_e"); f_av = C("f_av"); f_ev = C("f_ev"); f_tr = C("f_tr")
    w_l = C("w_l"); w_v = C("w_v"); w_s = C("w_s"); w_t = C("w_t"); w_ts = C("w_ts")
    wl4 = [C(f"wl4_{i}") for i in range(4)]
    ws2 = [C(f"ws2_{i}") for i in range(2)]
    wqr = [C(f"wqr_{i}") for i in range(2)]
    xo2 = [C(f"xo2_{i}") for i in range(2)]
    spl2 = [C(f"spl2_{i}") for i in range(2)]
    w1t2 = [C(f"w1t2_{i}") for i in range(2)]
    gqs2 = [C(f"gqs2_{i}") for i in range(2)]
    w2l2 = [C(f"w2l2_{i}") for i in range(2)]
    s_wo = C("s_wo"); s_y = C("s_y"); s_x2 = C("s_x2"); s_st2 = C("s_st2")
    s_xf = C("s_xf"); s_hq = C("s_hq"); s_spl = C("s_spl"); s_xo = C("s_xo")
    m_hqt = C("m_hqt"); m_w1t = C("m_w1t"); m_mm1 = C("m_mm1"); m_h1 = C("m_h1")
    m_g = C("m_g"); m_gq = C("m_gq"); m_gqs = C("m_gqs"); m_gqt = C("m_gqt")
    m_w2l = C("m_w2l"); m_mm2 = C("m_mm2"); m_h2 = C("m_h2"); m_tr = C("m_tr")
    m_trc = C("m_trc"); m_rl = C("m_rl"); m_out = C("m_out"); s_out = C("s_out")
    fence = C("fence")

    S0 = nc.alloc_psum_tensor("S0", [128, 1024], F32)
    S1 = nc.alloc_psum_tensor("S1", [128, 1024], F32)
    pb = [nc.alloc_psum_tensor(f"pb{i}", [128, 512], F32) for i in range(4)]
    S0h = [S0[:, 0:512], S0[:, 512:1024]]
    S1h = [S1[:, 0:512], S1[:, 512:1024]]
    S0b16 = S0[:].bitcast(BF16)
    S1b16 = S1[:].bitcast(BF16)

    A = nc.alloc_sbuf_tensor
    ones_c16 = A("ones_c16", [128, 1], BF16)
    ones_rf = A("ones_rf", [1, 128], F32)
    murow = A("murow", [1, 1024], F32)
    msrow = A("msrow", [1, 1024], F32)
    rstdrow = A("rstdrow", [1, 1024], F32)
    trowA = A("trowA", [1, 1024], F32)
    gq2 = A("gq2", [1, 1], F32)
    gk2 = A("gk2", [1, 1], F32)
    negMr = A("negMr", [1, 1], F32)
    negM = A("negM", [128, 1], F32)
    wbars = A("wbars", [128, 4], F32)
    id16 = A("id16", [128, 128], BF16)

    from contextlib import ExitStack
    g1 = ExitStack()
    _G = g1.enter_context
    q16 = _G(nc.sbuf_tensor("q16", [128, NHEAD * NT], BF16))
    k16 = _G(nc.sbuf_tensor("k16", [128, T], BF16))
    v16 = _G(nc.sbuf_tensor("v16", [128, T], BF16))
    v_km = _G(nc.sbuf_tensor("v_km", [128, 32 * 132], BF16))
    ao_sb = [_G(nc.sbuf_tensor(f"ao_sb{i}", [128, 128], BF16)) for i in range(4)]
    P_sb = [_G(nc.sbuf_tensor(f"P_sb{i}", [128, 1024], BF16)) for i in range(2)]
    wst = [_G(nc.sbuf_tensor(f"wst{i}", [128, 1024], F32)) for i in range(4)]
    wqo = [_G(nc.sbuf_tensor(f"wqo{i}", [128, 1024], BF16)) for i in range(2)]
    wq8 = [_G(nc.sbuf_tensor(f"wq8{i}", [128, 1024], I8)) for i in range(2)]
    wsc = [_G(nc.sbuf_tensor(f"wsc{i}", [128, 32], F32)) for i in range(2)]
    wmk = [_G(nc.sbuf_tensor(f"wmk{i}", [128, 32], F32)) for i in range(2)]
    wsl = [_G(nc.sbuf_tensor(f"wsl{i}", [128, 32], F32)) for i in range(2)]
    wrs = [_G(nc.sbuf_tensor(f"wrs{i}", [128, 32], F32)) for i in range(2)]
    rcp4 = _G(nc.sbuf_tensor("rcp4", [128, 4], F32))
    grt = ExitStack()
    aoT = None
    wo16 = None
    nonlocal_dbg45 = []

    def wch_src(i):
        if i < 128:
            a, hf = divmod(i, 2)
            return W1_v[:, a, hf * 1024:(hf + 1) * 1024]
        j = i - 128
        r, cc = j % 16, j // 16
        return W2_v[:, r, cc * 1024:(cc + 1) * 1024]

    def wch_dst(i):
        if i < 128:
            a, hf = divmod(i, 2)
            return W1q_v[:, a, hf * 1024:(hf + 1) * 1024]
        j = i - 128
        r, cc = j % 16, j // 16
        return W2q_v[:, r, cc * 1024:(cc + 1) * 1024]

    with nc.Block() as block:

        def emit_fence():
            n0 = fence.reserve(5)
            tgt = n0 + 5

            @block.sync
            def _(sync):
                sync.nop().then_inc(fence.h, 1)
                sync.wait_ge(fence.h, tgt)

            @block.gpsimd
            def _(gpsimd):
                gpsimd.engine_nop().then_inc(fence.h, 1)
                gpsimd.wait_ge(fence.h, tgt)

            @block.scalar
            def _(scalar):
                scalar.copy(negM[0:1, 0:1], negM[0:1, 0:1]).then_inc(fence.h, 1)
                scalar.wait_ge(fence.h, tgt)

            @block.vector
            def _(vector):
                vector.tensor_copy(wbars[0:1, 0:1], wbars[0:1, 0:1]).then_inc(fence.h, 1)
                vector.wait_ge(fence.h, tgt)

            @block.tensor
            def _(tensor):
                tensor.nop().then_inc(fence.h, 1)
                tensor.wait_ge(fence.h, tgt)

        @block.vector
        def _(vector):
            vector.memset(ones_c16[:], 1.0)
            vector.memset(ones_rf[:], 1.0)
            vector.memset(gq2[:], 0.0)
            vector.memset(gk2[:], 0.0)
            vector.memset(v_km[:], 1.0)

        @block.sync
        def _(sync):
            s_fld.incd(sync.dma_start(out=id16[:], in_=idn_d))
            for i in range(4):
                wl4[i].incd(sync.dma_start(out=wst[i][:], in_=wch_src(i)))

        ph1ctx = ExitStack()
        with ph1ctx:
            _E = ph1ctx.enter_context
            xT16 = _E(nc.sbuf_tensor("xT16", [128, 16 * 1024], BF16))
            wkv16 = _E(nc.sbuf_tensor("wkv16", [128, 16 * 256], BF16))
            wq16a = _E(nc.sbuf_tensor("wq16a", [128, 16 * 128], BF16))
            wq16b = _E(nc.sbuf_tensor("wq16b", [128, 16 * 128], BF16))
            sqb0 = _E(nc.sbuf_tensor("sqb0", [128, 512], BF16))
            sqb1 = _E(nc.sbuf_tensor("sqb1", [128, 512], BF16))
            muex = _E(nc.sbuf_tensor("muex", [128, 1024], F32))
            rsex = _E(nc.sbuf_tensor("rsex", [128, 1024], F32))
            evt0 = _E(nc.sbuf_tensor("evt0", [128, 512], F32))
            evt1 = _E(nc.sbuf_tensor("evt1", [128, 512], F32))
            wq16 = [wq16a, wq16b]
            sqb = [sqb0, sqb1]

            for Q in range(4):
                qc0 = Q * 1024
                xld_b = s_xld.reserve(32)
                qev_pre = s_qev.n

                @block.gpsimd
                def _(gpsimd, Q=Q, qc0=qc0, qev_pre=qev_pre):
                    if Q >= 1:
                        gpsimd.wait_ge(s_qev.h, qev_pre)
                    for part in range(2):
                        gpsimd.dma_start(
                            out=xT16[:].rearrange("p (d t) -> p d t", t=1024)[:, part * 8:(part + 1) * 8, :],
                            in_=xT_v[:, part * 8:(part + 1) * 8, qc0:qc0 + 1024]
                        ).then_inc(s_xld.h, 16)
                    if Q == 0:
                        s_wld.incd(gpsimd.dma_start(
                            out=wkv16[:].rearrange("p (d c) -> p d c", c=256), in_=wkv_v))
                        for i in range(2):
                            s_wld.incd(gpsimd.dma_start(
                                out=wq16[i][:].rearrange("p (d c) -> p d c", c=128),
                                in_=wq_v[:, :, i * 128:(i + 1) * 128]))

                sq_b = s_sq.reserve(32)
                smm_b = s_smm.reserve(32)
                vst_b = s_vst.reserve(3)

                @block.scalar
                def _(scalar, xld_b=xld_b, sq_b=sq_b, smm_b=smm_b):
                    scalar.wait_ge(s_xld.h, xld_b + 32)
                    for j in range(32):
                        ts2, dc = divmod(j, 16)
                        if j >= 2:
                            scalar.wait_ge(s_smm.h, smm_b + j - 1)
                        scalar.activation(
                            sqb[j % 2][:],
                            xT16[:, dc * 1024 + ts2 * 512: dc * 1024 + ts2 * 512 + 512],
                            Act.Square).then_inc(s_sq.h, 1)

                @block.tensor
                def _(tensor, Q=Q, xld_b=xld_b, sq_b=sq_b, smm_b=smm_b, qev_pre=qev_pre):
                    tensor.wait_ge(s_xld.h, xld_b + 32)
                    if Q >= 1:
                        tensor.wait_ge(s_qev.h, qev_pre)
                    for j in range(32):
                        ts2, dc = divmod(j, 16)
                        mu_p = pb[0] if ts2 == 0 else pb[2]
                        sq_p = pb[1] if ts2 == 0 else pb[3]
                        tensor.wait_ge(s_sq.h, sq_b + j + 1)
                        tensor.matmul(mu_p[0:1, :], ones_c16[:],
                                      xT16[:, dc * 1024 + ts2 * 512: dc * 1024 + ts2 * 512 + 512],
                                      start=(dc == 0), stop=(dc == 15))
                        tensor.matmul(sq_p[0:1, :], ones_c16[:], sqb[j % 2][:],
                                      start=(dc == 0), stop=(dc == 15)).then_inc(s_smm.h, 1)

                @block.vector
                def _(vector, smm_b=smm_b, vst_b=vst_b):
                    for ts2 in range(2):
                        mu_p = pb[0] if ts2 == 0 else pb[2]
                        sq_p = pb[1] if ts2 == 0 else pb[3]
                        vector.wait_ge(s_smm.h, smm_b + (ts2 + 1) * 16)
                        sl = slice(ts2 * 512, ts2 * 512 + 512)
                        vector.tensor_scalar(murow[0:1, sl], mu_p[0:1, :], 1.0 / D, None, Alu.mult)
                        vector.tensor_scalar(msrow[0:1, sl], sq_p[0:1, :], 1.0 / D, None,
                                             Alu.mult).then_inc(s_vst.h, 1)
                    sl = slice(0, 1024)
                    vector.tensor_mul(trowA[0:1, sl], murow[0:1, sl], murow[0:1, sl])
                    vector.tensor_sub(msrow[0:1, sl], msrow[0:1, sl], trowA[0:1, sl])
                    v_hz.inc(vector.tensor_scalar(msrow[0:1, sl], msrow[0:1, sl], EPS,
                                                  None, Alu.add))
                    v_hz.wait(vector)
                    _newton_rsqrt(vector, rstdrow[0:1, sl], msrow[0:1, sl],
                                  trowA[0:1, sl], v_hz).then_inc(s_vst.h, 1)

                exm_b = s_qkv.reserve(2)
                exp_b = s_exp.reserve(4)

                @block.tensor
                def _(tensor, vst_b=vst_b, exm_b=exm_b):
                    tensor.wait_ge(s_vst.h, vst_b + 3)
                    for s2 in range(2):
                        sl = slice(s2 * 512, s2 * 512 + 512)
                        mu_p = pb[0] if s2 == 0 else pb[2]
                        rs_p = pb[1] if s2 == 0 else pb[3]
                        tensor.matmul(mu_p[:], ones_rf[:], murow[0:1, sl], start=True, stop=True)
                        tensor.matmul(rs_p[:], ones_rf[:], rstdrow[0:1, sl],
                                      start=True, stop=True).then_inc(s_qkv.h, 1)

                @block.scalar
                def _(scalar, exm_b=exm_b, exp_b=exp_b):
                    for s2 in range(2):
                        scalar.wait_ge(s_qkv.h, exm_b + s2 + 1)
                        mu_p = pb[0] if s2 == 0 else pb[2]
                        rs_p = pb[1] if s2 == 0 else pb[3]
                        scalar.copy(muex[:, s2 * 512:(s2 + 1) * 512], mu_p[:]).then_inc(s_exp.h, 1)
                        scalar.copy(rsex[:, s2 * 512:(s2 + 1) * 512], rs_p[:]).then_inc(s_exp.h, 1)

                ng = 18 if Q == 0 else 2
                mm_b = s_qkv.reserve(ng)
                sc_b = s_sc.reserve(ng)
                ev_b = s_qev.reserve(ng)

                @block.tensor
                def _(tensor, Q=Q, ng=ng, exp_b=exp_b, mm_b=mm_b, ev_b=ev_b):
                    tensor.wait_ge(s_exp.h, exp_b + 4)
                    if Q == 0:
                        tensor.wait_ge(s_wld.h, 48)
                    for gi in range(ng):
                        if gi >= 1:
                            tensor.wait_ge(s_qev.h, ev_b + gi)
                        if gi >= 4:
                            oc = gi - 2
                            p = oc % 2
                            tensor.wait_ge(wqr[p].h, 16 * ((oc - 4 - p) // 2 + 1))
                        for dc in range(16):
                            if gi < 2:
                                lhs = wkv16[:, dc * 256 + gi * 128: dc * 256 + gi * 128 + 128]
                            else:
                                oc = gi - 2
                                lhs = wq16[oc % 2][:, dc * 128:(dc + 1) * 128]
                            tensor.matmul(pb[0][:], lhs, xT16[:, dc * 1024: dc * 1024 + 512],
                                          start=(dc == 0), stop=(dc == 15))
                            tensor.matmul(pb[1][:], lhs, xT16[:, dc * 1024 + 512: dc * 1024 + 1024],
                                          start=(dc == 0), stop=(dc == 15))
                            mm = tensor.matmul(S0[:, 0:1], lhs, ones_c16[:],
                                               start=(dc == 0), stop=(dc == 15))
                        mm.then_inc(s_qkv.h, 1)

                @block.scalar
                def _(scalar, ng=ng, mm_b=mm_b, sc_b=sc_b):
                    for gi in range(ng):
                        scalar.wait_ge(s_qkv.h, mm_b + gi + 1)
                        scalar.copy(wbars[:, gi % 4:gi % 4 + 1], S0[:, 0:1]).then_inc(s_sc.h, 1)

                @block.gpsimd
                def _(gpsimd, Q=Q, mm_b=mm_b):
                    if Q == 0:
                        for oc in range(14):
                            gpsimd.wait_ge(s_qkv.h, mm_b + 2 + oc + 1)
                            wqr[oc % 2].incd(gpsimd.dma_start(
                                out=wq16[oc % 2][:].rearrange("p (d c) -> p d c", c=128),
                                in_=wq_v[:, :, (oc + 2) * 128:(oc + 3) * 128]))

                @block.vector
                def _(vector, Q=Q, ng=ng, qc0=qc0, sc_b=sc_b, ev_b=ev_b):
                    for gi in range(ng):
                        vector.wait_ge(s_sc.h, sc_b + gi + 1)
                        for s2 in range(2):
                            sl = slice(s2 * 512, s2 * 512 + 512)
                            vector.tensor_scalar(evt0[:], muex[:, sl],
                                                 wbars[:, gi % 4:gi % 4 + 1], None, Alu.mult)
                            vector.tensor_sub(evt1[:], pb[s2][:], evt0[:])
                            if gi < 2:
                                dst = k16 if gi == 0 else v16
                                last = vector.tensor_tensor(dst[:, qc0 + sl.start: qc0 + sl.stop],
                                                            evt1[:], rsex[:, sl], Alu.mult)
                            else:
                                oc = gi - 2
                                last = vector.tensor_tensor(
                                    q16[:, oc * NT + sl.start: oc * NT + sl.stop],
                                    evt1[:], rsex[:, sl], Alu.mult)
                        last.then_inc(s_qev.h, 1)

        emit_fence()

        if STAGE >= 2:
            ph2ctx = ExitStack()
            with ph2ctx:
                _E = ph2ctx.enter_context
                ckmap = _E(nc.sbuf_tensor("ckmap", [128, T], F32))
                skmap = _E(nc.sbuf_tensor("skmap", [128, T], F32))
                rotb0 = _E(nc.sbuf_tensor("rotb0", [128, 1024], BF16))
                rotb1 = _E(nc.sbuf_tensor("rotb1", [128, 1024], BF16))
                rtmp0 = _E(nc.sbuf_tensor("rtmp0", [128, 1024], F32))
                rtmp1 = _E(nc.sbuf_tensor("rtmp1", [128, 1024], F32))
                sqn0 = _E(nc.sbuf_tensor("sqn0", [128, 1024], BF16))
                sqn1 = _E(nc.sbuf_tensor("sqn1", [128, 1024], BF16))
                nrow = _E(nc.sbuf_tensor("nrow", [1, 1], F32))
                srowD = _E(nc.sbuf_tensor("srowD", [1, 1024], F32))
                sqnD = _E(nc.sbuf_tensor("sqnD", [128, 1024], F32))
                rotb = [rotb0, rotb1]
                sqn = [sqn0, sqn1]

                fld_b = s_fld.reserve(32)

                @block.sync
                def _(sync):
                    sync.dma_start(out=ckmap[:], in_=cosk).then_inc(s_fld.h, 16)
                    sync.dma_start(out=skmap[:], in_=sink).then_inc(s_fld.h, 16)

                ROPES = [("q", h) for h in range(NHEAD)] + [("k", i) for i in range(4)]

                def rsrc(kind, i):
                    return (q16[:, i * NT:(i + 1) * NT] if kind == "q"
                            else k16[:, i * 1024:(i + 1) * 1024])

                def rmap(kind, i, mp):
                    return mp[:, 0:1024] if kind == "q" else mp[:, i * 1024:(i + 1) * 1024]

                rot_b = s_rot.reserve(32 * 20)
                rope_b = s_rope.reserve(20)

                @block.gpsimd
                def _(gpsimd):
                    for j, (kind, i) in enumerate(ROPES):
                        src = rsrc(kind, i)
                        if j >= 2:
                            gpsimd.wait_ge(s_rope.h, rope_b + j - 1)
                        gpsimd.dma_start(out=rotb[j % 2][0:64, :],
                                         in_=src[64:128, :]).then_inc(s_rot.h, 16)
                        gpsimd.dma_start(out=rotb[j % 2][64:128, :],
                                         in_=src[0:64, :]).then_inc(s_rot.h, 16)

                @block.vector
                def _(vector):
                    vector.wait_ge(s_fld.h, fld_b + 32)
                    for j, (kind, i) in enumerate(ROPES):
                        src = rsrc(kind, i)
                        vector.wait_ge(s_rot.h, rot_b + 32 * (j + 1))
                        vector.tensor_tensor(rtmp0[:], src, rmap(kind, i, ckmap), Alu.mult)
                        vector.tensor_tensor(rtmp1[:], rotb[j % 2][:], rmap(kind, i, skmap), Alu.mult)
                        vector.tensor_add(src, rtmp0[:], rtmp1[:]).then_inc(s_rope.h, 1)

                vt_b = s_vt.reserve(32)
                vtc_b = s_vtc.reserve(32)

                @block.tensor
                def _(tensor):
                    s_fld.wait(tensor, fld_b)
                    s_qev.wait(tensor)
                    for kc in range(32):
                        if kc >= 2:
                            tensor.wait_ge(s_vtc.h, vtc_b + kc - 1)
                        tensor.transpose(S0b16[:, (kc % 2) * 1024:(kc % 2) * 1024 + 128],
                                         v16[:, kc * 128:(kc + 1) * 128],
                                         id16[:]).then_inc(s_vt.h, 1)

                @block.scalar
                def _(scalar):
                    for kc in range(32):
                        scalar.wait_ge(s_vt.h, vt_b + kc + 1)
                        scalar.copy(v_km[:, kc * 132: kc * 132 + 128],
                                    S0b16[:, (kc % 2) * 1024:(kc % 2) * 1024 + 128]).then_inc(s_vtc.h, 1)

                nsq_b = s_nsq.reserve(20)
                nmm_b = s_nmm.reserve(20)
                nrm_b = s_nrm.reserve(21)

                @block.scalar
                def _(scalar):
                    for j, (kind, i) in enumerate(ROPES):
                        scalar.wait_ge(s_rope.h, rope_b + j + 1)
                        if j >= 2:
                            scalar.wait_ge(s_nmm.h, nmm_b + j - 1)
                        scalar.activation(sqn[j % 2][:], rsrc(kind, i),
                                          Act.Square).then_inc(s_nsq.h, 1)

                @block.tensor
                def _(tensor):
                    s_vtc.wait(tensor)
                    for j in range(20):
                        tensor.wait_ge(s_nsq.h, nsq_b + j + 1)
                        if j >= 2:
                            tensor.wait_ge(s_nrm.h, nrm_b + j - 1)
                        Sj = S0 if j % 2 == 0 else S1
                        tensor.matmul(Sj[0:1, 0:512], ones_c16[:], sqn[j % 2][:, 0:512],
                                      start=True, stop=True)
                        tensor.matmul(Sj[0:1, 512:1024], ones_c16[:], sqn[j % 2][:, 512:1024],
                                      start=True, stop=True).then_inc(s_nmm.h, 1)

                @block.vector
                def _(vector):
                    for j, (kind, i) in enumerate(ROPES):
                        vector.wait_ge(s_nmm.h, nmm_b + j + 1)
                        Sj = S0 if j % 2 == 0 else S1
                        if j == 0:
                            vector.tensor_copy(srowD[:], Sj[0:1, 0:1024])
                            vector.tensor_copy(sqnD[:], sqn[j % 2][:])
                        v_hz.inc(vector.tensor_reduce(nrow[0:1, 0:1], Sj[0:1, 0:1024],
                                                      AX.X, Alu.max))
                        v_hz.wait(vector)
                        acc = gq2 if kind == "q" else gk2
                        vector.tensor_max(acc[:], acc[:],
                                          nrow[0:1, 0:1]).then_inc(s_nrm.h, 1)
                    v_hz.inc(vector.tensor_copy(nrow[0:1, 0:1], nrow[0:1, 0:1]))
                    v_hz.wait(vector)
                    v_hz.inc(vector.tensor_add(negMr[:], gq2[:], gk2[:]))
                    v_hz.wait(vector)
                    vector.tensor_scalar(negMr[:], negMr[:], -0.5 * CSC, None,
                                         Alu.mult).then_inc(s_nrm.h, 1)

                if STAGE < 5:
                    dbg_b = s_out.reserve(4 * 16)

                    @block.gpsimd
                    def _(gpsimd):
                        gpsimd.wait_ge(s_nrm.h, nrm_b + 21)
                        gpsimd.dma_start(out=dbg["gq2"], in_=gq2[:]).then_inc(s_out.h, 16)
                        gpsimd.dma_start(out=dbg["gk2"], in_=gk2[:]).then_inc(s_out.h, 16)
                        gpsimd.dma_start(out=dbg["srow"], in_=srowD[:]).then_inc(s_out.h, 16)
                        gpsimd.dma_start(out=dbg["sqnl"], in_=sqnD[:]).then_inc(s_out.h, 16)
                        gpsimd.wait_ge(s_out.h, dbg_b + 64)

                ngb_b = s_out.reserve(8 * 16)
                sc2_b = s_sc.reserve(1)

                @block.gpsimd
                def _(gpsimd):
                    gpsimd.wait_ge(s_nrm.h, nrm_b + 21)
                    gpsimd.dma_start(out=negM[0:1, :], in_=negMr[0:1, :]).then_inc(s_out.h, 16)
                    n = 1
                    step = 1
                    while n < 128:
                        gpsimd.wait_ge(s_out.h, ngb_b + 16 * step)
                        m = min(n, 128 - n)
                        gpsimd.dma_start(out=negM[n:n + m, :], in_=negM[0:m, :]).then_inc(s_out.h, 16)
                        n += m
                        step += 1

                @block.scalar
                def _(scalar):
                    scalar.wait_ge(s_out.h, ngb_b + 8 * 16)
                    scalar.nop().then_inc(s_sc.h, 1)

            emit_fence()

        if STAGE >= 3:
            aoT = grt.enter_context(nc.sbuf_tensor("aoT", [128, NHEAD * NT], BF16, side="right"))
            wo16 = grt.enter_context(nc.sbuf_tensor("wo16", [128, 16 * 2048], BF16, side="right"))
            av = [pb[i][:, 0:129] for i in range(4)]
            Ss = [S0, S1]
            qk_b = f_qk.reserve(32 * 16)
            e_b = f_e.reserve(32 * 16)
            av_b = f_av.reserve(32 * 16)
            ev_b = f_ev.reserve(32)
            tr_b = f_tr.reserve(32 * 64)
            wl_b = 0
            w_l.reserve(16 * 252)
            wv_b = w_v.reserve(256)
            ws_b = w_s.reserve(16 * 256)
            wo_b = s_wo.reserve(32)

            @block.tensor
            def _(tensor):
                s_sc.wait(tensor)
                s_vtc.wait(tensor)
                s_rope.wait(tensor)
                for B in range(32):
                    h, qb = divmod(B, 2)
                    qh = q16[:, h * NT + qb * 512: h * NT + qb * 512 + 512]

                    def emit_qk(j, B=B, qh=qh):
                        ej = 16 * B + j - 1
                        if ej > 0:
                            tensor.wait_ge(f_e.h, e_b + ej)
                        for u in range(2):
                            kc = 2 * j + u
                            mm = tensor.matmul(Ss[j % 2][:, u * 512:(u + 1) * 512],
                                               k16[:, kc * 128:(kc + 1) * 128], qh,
                                               start=True, stop=True)
                        mm.then_inc(f_qk.h, 1)

                    emit_qk(0)
                    emit_qk(1)
                    for j in range(16):
                        tensor.wait_ge(f_e.h, e_b + 16 * B + j + 1)
                        if j == 0 and B > 0:
                            tensor.wait_ge(f_ev.h, ev_b + B)
                        for u in range(2):
                            kc = 2 * j + u
                            for qc in range(4):
                                mm = tensor.matmul(
                                    av[qc],
                                    P_sb[j % 2][:, u * 512 + qc * 128: u * 512 + qc * 128 + 128],
                                    v_km[:, kc * 132: kc * 132 + 129],
                                    start=(kc == 0), stop=(kc == 31))
                        mm.then_inc(f_av.h, 1)
                        if j < 14:
                            emit_qk(j + 2)

            @block.scalar
            def _(scalar):
                s_sc.wait(scalar)
                for B in range(32):
                    for j in range(16):
                        scalar.wait_ge(f_qk.h, qk_b + 16 * B + j + 1)
                        aj = 16 * B + j - 1
                        if aj > 0:
                            scalar.wait_ge(f_av.h, av_b + aj)
                        scalar.activation(P_sb[j % 2][:], Ss[j % 2][:], Act.Exp,
                                          bias=negM[:], scale=CSC).then_inc(f_e.h, 1)

            @block.vector
            def _(vector):
                for B in range(32):
                    vector.wait_ge(f_av.h, av_b + 16 * (B + 1))
                    if B > 0:
                        vector.wait_ge(f_tr.h, tr_b + 64 * B)
                    for qc in range(4):
                        r_last = vector.reciprocal(rcp4[:, qc:qc + 1], av[qc][:, 128:129])
                    v_hz.inc(r_last)
                    v_hz.wait(vector)
                    for qc in range(4):
                        last = vector.tensor_scalar(ao_sb[qc][:], av[qc][:, 0:128],
                                                    rcp4[:, qc:qc + 1], None, Alu.mult)
                    last.then_inc(f_ev.h, 1)
                    for i in range(8 * B, 8 * B + 8):
                        vector.wait_ge(wl4[i % 4].h, 16 * (i // 4 + 1))
                        if i >= 2:
                            vector.wait_ge(ws2[i % 2].h, 16 * (i // 2))
                        _quant_ops(vector, wst[i % 4][:], 32,
                                   wsc[i % 2][:], wmk[i % 2][:], wsl[i % 2][:], wrs[i % 2][:],
                                   wq8[i % 2][:], wqo[i % 2][:], v_hz).then_inc(w_v.h, 1)

            @block.sync
            def _(sync):
                for B in range(32):
                    sync.wait_ge(f_ev.h, ev_b + B + 1)
                    h, qb = divmod(B, 2)
                    for qc in range(4):
                        sync.dma_start_transpose(
                            aoT[:, h * NT + qb * 512 + qc * 128: h * NT + qb * 512 + qc * 128 + 128],
                            ao_sb[qc][:]).then_inc(f_tr.h, 16)
                    for i in range(8 * B + 4, min(8 * B + 12, 256)):
                        if i - 3 > 0:
                            sync.wait_ge(w_v.h, wv_b + i - 3)
                        wl4[i % 4].incd(sync.dma_start(out=wst[i % 4][:], in_=wch_src(i)))

            @block.gpsimd
            def _(gpsimd):
                for part in range(2):
                    gpsimd.dma_start(
                        out=wo16[:].rearrange("p (d c) -> p d c", c=2048)[:, part * 8:(part + 1) * 8, :],
                        in_=wo_v[:, part * 8:(part + 1) * 8, :]).then_inc(s_wo.h, 16)
                for i in range(256):
                    gpsimd.wait_ge(w_v.h, wv_b + i + 1)
                    ws2[i % 2].incd(gpsimd.dma_start(out=wch_dst(i), in_=wqo[i % 2][:]))
                    w_s.n += 16

            emit_fence()
            if STAGE >= 4:
                g1.close()

        if STAGE >= 4:
            ph45ctx = ExitStack()
            with ph45ctx:
                _E = ph45ctx.enter_context
                x2ch0 = _E(nc.sbuf_tensor("x2ch0", [128, 2048], F32))
                x2ch1 = _E(nc.sbuf_tensor("x2ch1", [128, 2048], F32))
                xoch0 = _E(nc.sbuf_tensor("xoch0", [128, 2048], F32))
                xoch1 = _E(nc.sbuf_tensor("xoch1", [128, 2048], F32))
                xfch0 = _E(nc.sbuf_tensor("xfch0", [128, 2048], BF16))
                xfch1 = _E(nc.sbuf_tensor("xfch1", [128, 2048], BF16))
                hch0 = _E(nc.sbuf_tensor("hch0", [128, 2048], BF16))
                hch1 = _E(nc.sbuf_tensor("hch1", [128, 2048], BF16))
                htmp = _E(nc.sbuf_tensor("htmp", [128, 2048], F32))
                sjunk = _E(nc.sbuf_tensor("sjunk", [128, 2048], BF16))
                lnwe = _E(nc.sbuf_tensor("lnwe", [128, 2048], F32))
                lnbe = _E(nc.sbuf_tensor("lnbe", [128, 2048], F32))
                stats = _E(nc.sbuf_tensor("stats", [128, 16], F32))
                hq8 = _E(nc.sbuf_tensor("hq8", [128, 2048], I8))
                hamx = _E(nc.sbuf_tensor("hamx", [128, 64], F32))
                hscl = _E(nc.sbuf_tensor("hscl", [128, 64], F32))
                hrcl = _E(nc.sbuf_tensor("hrcl", [128, 64], F32))
                hmsk = _E(nc.sbuf_tensor("hmsk", [128, 64], F32))
                sacca = _E(nc.sbuf_tensor("sacca", [128, 1], F32))
                saccb = _E(nc.sbuf_tensor("saccb", [128, 1], F32))
                ssum = _E(nc.sbuf_tensor("ssum", [128, 1], F32))
                smu2 = _E(nc.sbuf_tensor("smu2", [128, 1], F32))
                svar = _E(nc.sbuf_tensor("svar", [128, 1], F32))
                srstd2 = _E(nc.sbuf_tensor("srstd2", [128, 1], F32))
                smu3 = _E(nc.sbuf_tensor("smu3", [128, 1], F32))
                srstd3 = _E(nc.sbuf_tensor("srstd3", [128, 1], F32))
                sntmp = _E(nc.sbuf_tensor("sntmp", [128, 1], F32))
                nonlocal_dbg45.extend([stats, sacca, saccb])
                w2ta = _E(nc.sbuf_tensor("w2ta", [128, 2048], BF16))
                w2tb = _E(nc.sbuf_tensor("w2tb", [128, 2048], BF16))
                x2ch = [x2ch0, x2ch1]
                xoch = [xoch0, xoch1]
                xfch = [xfch0, xfch1]
                hch = [hch0, hch1]
                w2t_st = [w2ta, w2tb]

                fld_b = s_fld.reserve(32)

                @block.sync
                def _(sync):
                    sync.dma_start(out=lnwe[:], in_=lnwx).then_inc(s_fld.h, 16)
                    sync.dma_start(out=lnbe[:], in_=lnbx).then_inc(s_fld.h, 16)

                xo_b = s_xo.reserve(16 * 8)
                y_b = s_y.reserve(8)
                x2_b = s_x2.reserve(8)
                st2_b = s_st2.reserve(16)
                xf_b = s_xf.reserve(8)
                hq_b = s_hq.reserve(8)
                spl_b = s_spl.reserve(16 * 24)
                wt_b = w_t.reserve(16 * 64)
                wts_b = w_ts.reserve(16 * 64)

                @block.sync
                def _(sync):
                    for tc in range(8):
                        if tc >= 2:
                            sync.wait_ge(s_x2.h, x2_b + tc - 1)
                        xo2[tc % 2].incd(sync.dma_start(out=xoch[tc % 2][:], in_=xov_v[:, tc, :]))

                @block.tensor
                def _(tensor):
                    s_wo.wait(tensor)
                    f_tr.wait(tensor)
                    for tc in range(8):
                        dst = [S0h[0], S0h[1], S1h[0], S1h[1]] if tc % 2 == 0 else [p[:] for p in pb]
                        if tc >= 2:
                            tensor.wait_ge(s_x2.h, x2_b + tc - 1)
                        for dc in range(16):
                            lhs = aoT[:, dc * NT + tc * 128: dc * NT + tc * 128 + 128]
                            for s4 in range(4):
                                mm = tensor.matmul(dst[s4], lhs,
                                                   wo16[:, dc * 2048 + s4 * 512: dc * 2048 + s4 * 512 + 512],
                                                   start=(dc == 0), stop=(dc == 15))
                        mm.then_inc(s_y.h, 1)

                @block.vector
                def _(vector):
                    for tc in range(8):
                        vector.wait_ge(s_y.h, y_b + tc + 1)
                        vector.wait_ge(xo2[tc % 2].h, 16 * (tc // 2 + 1))
                        if tc >= 2:
                            vector.wait_ge(spl2[tc % 2].h, 48 * (tc // 2))
                        if tc == 0:
                            vector.wait_ge(s_fld.h, fld_b + 32)
                        if tc % 2 == 0:
                            vector.tensor_add(x2ch[0][:, 0:1024], S0[:], xoch[0][:, 0:1024])
                            vector.tensor_add(x2ch[0][:, 1024:2048], S1[:], xoch[0][:, 1024:2048])
                        else:
                            for s4 in range(4):
                                vector.tensor_add(x2ch[1][:, s4 * 512:(s4 + 1) * 512], pb[s4][:],
                                                  xoch[1][:, s4 * 512:(s4 + 1) * 512])
                        vector.tensor_reduce(ssum[:], x2ch[tc % 2][:], AX.X,
                                             Alu.add).then_inc(s_x2.h, 1)
                        if PH45_TRIM:
                            vector.wait_ge(s_st2.h, st2_b + 2 * tc + 1)
                            vector.tensor_copy(xfch[tc % 2][:], x2ch[tc % 2][:]).then_inc(s_xf.h, 1)
                            vector.wait_ge(s_st2.h, st2_b + 2 * tc + 2)
                            vector.tensor_copy(hch[tc % 2][:], xfch[tc % 2][:]).then_inc(s_hq.h, 1)
                            continue
                        vector.wait_ge(s_st2.h, st2_b + 2 * tc + 1)
                        vector.tensor_scalar(smu2[:], ssum[:], 1.0 / D, None, Alu.mult)
                        vector.tensor_scalar(svar[:], sacca[:], 1.0 / D, None, Alu.mult)
                        vector.tensor_mul(sntmp[:], smu2[:], smu2[:])
                        vector.tensor_sub(svar[:], svar[:], sntmp[:])
                        v_hz.inc(vector.tensor_scalar(svar[:], svar[:], EPS, None, Alu.add))
                        v_hz.wait(vector)
                        v_hz.inc(_newton_rsqrt(vector, srstd2[:], svar[:], sntmp[:], v_hz,
                                               paranoid=True))
                        v_hz.wait(vector)
                        vector.tensor_scalar(xfch[tc % 2][:], x2ch[tc % 2][:],
                                             smu2[:], srstd2[:], Alu.subtract, Alu.mult)
                        vector.tensor_reduce(ssum[:], xfch[tc % 2][:], AX.X,
                                             Alu.add).then_inc(s_xf.h, 1)
                        vector.wait_ge(s_st2.h, st2_b + 2 * tc + 2)
                        vector.tensor_scalar(smu3[:], ssum[:], 1.0 / D, None, Alu.mult)
                        vector.tensor_scalar(svar[:], saccb[:], 1.0 / D, None, Alu.mult)
                        vector.tensor_mul(sntmp[:], smu3[:], smu3[:])
                        vector.tensor_sub(svar[:], svar[:], sntmp[:])
                        v_hz.inc(vector.tensor_scalar(svar[:], svar[:], EPS, None, Alu.add))
                        v_hz.wait(vector)
                        v_hz.inc(_newton_rsqrt(vector, srstd3[:], svar[:], sntmp[:], v_hz,
                                               paranoid=True))
                        v_hz.wait(vector)
                        vector.tensor_scalar(htmp[:], xfch[tc % 2][:],
                                             smu3[:], srstd3[:], Alu.subtract, Alu.mult)
                        vector.tensor_mul(htmp[:], htmp[:], lnwe[:])
                        v_hz.inc(vector.tensor_add(hch[tc % 2][:], htmp[:], lnbe[:]))
                        v_hz.wait(vector)
                        _quant_ops(vector, hch[tc % 2][:], 64, hamx[:], hmsk[:], hscl[:],
                                   hrcl[:], hq8[:], hch[tc % 2][:], v_hz).then_inc(s_hq.h, 1)

                @block.scalar
                def _(scalar):
                    for tc in range(8):
                        scalar.wait_ge(s_x2.h, x2_b + tc + 1)
                        scalar.activation(sjunk[:], x2ch[tc % 2][:], Act.Square,
                                          accum_out=sacca[:]).then_inc(s_st2.h, 1)
                        scalar.wait_ge(s_xf.h, xf_b + tc + 1)
                        scalar.activation(sjunk[:], xfch[tc % 2][:], Act.Square,
                                          accum_out=saccb[:]).then_inc(s_st2.h, 1)
                    if STAGE >= 5:
                        scalar.wait_ge(ws2[0].h, 16 * 128)
                        scalar.wait_ge(ws2[1].h, 16 * 128)
                        for hc in range(64):
                            if hc >= 2:
                                scalar.wait_ge(w_ts.h, wts_b + 16 * (hc - 1))
                            scalar.dma_start_transpose(
                                w2t_st[hc % 2][:],
                                W2q[:, hc * 128:(hc + 1) * 128]).then_inc(w_t.h, 16)

                @block.gpsimd
                def _(gpsimd):
                    for tc in range(8):
                        gpsimd.wait_ge(s_hq.h, hq_b + tc + 1)
                        spl2[tc % 2].incd(gpsimd.dma_start(out=hq_v[:, tc, :], in_=hch[tc % 2][:]))
                        spl2[tc % 2].incd(gpsimd.dma_start(out=x2_v[:, tc, :], in_=x2ch[tc % 2][:]))
                        spl2[tc % 2].incd(gpsimd.dma_start(out=xf_v[:, tc, :], in_=xfch[tc % 2][:]))
                    if STAGE >= 5:
                        for hc in range(64):
                            gpsimd.wait_ge(w_t.h, wt_b + 16 * (hc + 1))
                            gpsimd.dma_start(out=W2qT_v[:, hc, :],
                                             in_=w2t_st[hc % 2][:]).then_inc(w_ts.h, 16)

            emit_fence()

        if STAGE >= 5:
            grt.close()
            ph6ctx = ExitStack()
            with ph6ctx:
                _E = ph6ctx.enter_context
                hqT = _E(nc.sbuf_tensor("hqT", [128, 16 * 512], BF16))
                gqT = _E(nc.sbuf_tensor("gqT", [128, 64 * 512], BF16))
                wstg0 = _E(nc.sbuf_tensor("wstg0", [128, 8192], BF16))
                wstg1 = _E(nc.sbuf_tensor("wstg1", [128, 8192], BF16))
                h1b0 = _E(nc.sbuf_tensor("h1b0", [128, 512], BF16))
                h1b1 = _E(nc.sbuf_tensor("h1b1", [128, 512], BF16))
                gb0 = _E(nc.sbuf_tensor("gb0", [128, 512], BF16))
                gb1 = _E(nc.sbuf_tensor("gb1", [128, 512], BF16))
                gq80 = _E(nc.sbuf_tensor("gq80", [128, 512], I8))
                gq81 = _E(nc.sbuf_tensor("gq81", [128, 512], I8))
                gqo0 = _E(nc.sbuf_tensor("gqo0", [128, 512], BF16))
                gqo1 = _E(nc.sbuf_tensor("gqo1", [128, 512], BF16))
                gam0 = _E(nc.sbuf_tensor("gam0", [128, 16], F32))
                gam1 = _E(nc.sbuf_tensor("gam1", [128, 16], F32))
                grc0 = _E(nc.sbuf_tensor("grc0", [128, 16], F32))
                grc1 = _E(nc.sbuf_tensor("grc1", [128, 16], F32))
                gms0 = _E(nc.sbuf_tensor("gms0", [128, 16], F32))
                gms1 = _E(nc.sbuf_tensor("gms1", [128, 16], F32))
                gsl0 = _E(nc.sbuf_tensor("gsl0", [128, 16], F32))
                gsl1 = _E(nc.sbuf_tensor("gsl1", [128, 16], F32))
                b1hi = _E(nc.sbuf_tensor("b1hi", [1, HID], BF16))
                b2sb = _E(nc.sbuf_tensor("b2sb", [128, 16], F32))
                ones1r = _E(nc.sbuf_tensor("ones1r", [1, 128], BF16))
                h2T = _E(nc.sbuf_tensor("h2T", [128, 16 * 512], BF16))
                xfrl2 = [_E(nc.sbuf_tensor(f"xfrl{i}", [128, 2048], BF16)) for i in range(2)]
                x2rl2 = [_E(nc.sbuf_tensor(f"x2rl{i}", [128, 2048], F32)) for i in range(2)]
                h1b = [h1b0, h1b1]
                gbv = [gb0, gb1]
                gq8v = [gq80, gq81]
                gqov = [gqo0, gqo1]
                gamv = [gam0, gam1]
                grcv = [grc0, grc1]
                gmsv = [gms0, gms1]
                gslv = [gsl0, gsl1]
                wstg = [wstg0, wstg1]

                fld6 = s_fld.reserve(32)

                @block.gpsimd
                def _(gpsimd):
                    gpsimd.dma_start(out=b1hi[:], in_=b1dr).then_inc(m_rl.h, 16)

                @block.sync
                def _(sync):
                    sync.dma_start(out=b2sb[:], in_=b2c).then_inc(s_fld.h, 16)
                    sync.wait_ge(spl2[0].h, 48 * 4)
                    sync.wait_ge(spl2[1].h, 48 * 4)

                rl0 = m_rl.reserve(16)

                @block.vector
                def _(vector):
                    vector.memset(ones1r[:], 1.0)

                for half in range(2):
                    t0 = half * 4
                    mm1_prev_end = m_mm1.n
                    h1_prev_end = m_h1.n
                    mm2_prev_end = m_mm2.n
                    h2_prev_end = m_h2.n
                    out_prev_end = m_out.n
                    so_prev_end = s_out.n
                    w1t_b = m_w1t.reserve(16 * 256)
                    w1t_hbase = [w1t2[0].n, w1t2[1].n]
                    mm1_b = m_mm1.reserve(64)
                    h1_b = m_h1.reserve(64)
                    gg_b = m_g.reserve(64)
                    gqq_b = m_gq.reserve(64)
                    gqs_hbase = [gqs2[0].n, gqs2[1].n]
                    gqt_b = m_gqt.reserve(16 * 64)
                    if PH6_STOP != 1:
                        w2l_b = m_w2l.reserve(16 * 16)
                        w2l_hbase = [w2l2[0].n, w2l2[1].n]
                        mm2_b = m_mm2.reserve(16)
                        h2_b = m_h2.reserve(16)
                        if PH6_STOP != 2:
                            tr_b6 = m_tr.reserve(64)
                            trc_b = m_trc.reserve(4)
                            rl_b = m_rl.reserve(16 * 8)
                            out_b = m_out.reserve(4)
                            so_b = s_out.reserve(16 * 4)

                    hqt_b = m_hqt.reserve(16 * 16)

                    @block.sync
                    def _(sync, half=half, w1t_b=w1t_b, mm1_b=mm1_b,
                          mm1_prev_end=mm1_prev_end, mm2_prev_end=mm2_prev_end):
                        if half == 1:
                            sync.wait_ge(m_mm1.h, mm1_prev_end)
                            if PH6_STOP != 1:
                                sync.wait_ge(m_mm2.h, mm2_prev_end)
                        for dc in range(16):
                            sync.dma_start_transpose(
                                hqT[:, dc * 512:(dc + 1) * 512],
                                hq_d[half * 512:(half + 1) * 512,
                                     dc * 128:(dc + 1) * 128]).then_inc(m_hqt.h, 16)
                        for ht in range(16):
                            if ht >= 2:
                                sync.wait_ge(m_mm1.h, mm1_b + 4 * (ht - 1))
                            if half == 0 and ht == 0:
                                sync.wait_ge(ws2[0].h, 16 * 64)
                                sync.wait_ge(ws2[1].h, 16 * 64)
                            for dc in range(16):
                                w1t2[ht % 2].incd(sync.dma_start_transpose(
                                    wstg[ht % 2][:, dc * 512:(dc + 1) * 512],
                                    W1q[ht * 512:(ht + 1) * 512, dc * 128:(dc + 1) * 128]))

                    @block.tensor
                    def _(tensor, half=half, t0=t0, w1t_b=w1t_b, mm1_b=mm1_b, h1_b=h1_b,
                          h1_prev_end=h1_prev_end, h2_prev_end=h2_prev_end,
                          out_prev_end=out_prev_end, hqt_b=hqt_b):
                        if half == 0:
                            m_rl.wait(tensor, rl0 + 16)
                        elif PH6_STOP == 0:
                            m_out.wait(tensor, out_prev_end)
                        elif PH6_STOP == 2:
                            tensor.wait_ge(m_h2.h, h2_prev_end)
                        else:
                            tensor.wait_ge(m_h1.h, h1_prev_end)
                        tensor.wait_ge(m_hqt.h, hqt_b + 256)
                        for ht in range(16):
                            tensor.wait_ge(w1t2[ht % 2].h, 256 * (ht // 2 + 1) + (256 * 128 if False else 0) + w1t_hbase[ht % 2])
                            acc = [S0h[0], S0h[1], S1h[0], S1h[1]] if ht % 2 == 0 else [p[:] for p in pb]
                            if ht >= 2:
                                tensor.wait_ge(m_h1.h, h1_b + 4 * (ht - 1))
                            for dc in range(16):
                                for tc in range(4):
                                    tensor.matmul(
                                        acc[tc],
                                        hqT[:, dc * 512 + tc * 128: dc * 512 + tc * 128 + 128],
                                        wstg[ht % 2][:, dc * 512:(dc + 1) * 512],
                                        start=(dc == 0), stop=False)
                            for tc in range(4):
                                tensor.matmul(acc[tc], ones1r[:], b1hi[0:1, ht * 512:(ht + 1) * 512],
                                              start=False, stop=True).then_inc(m_mm1.h, 1)

                    @block.scalar
                    def _(scalar, half=half, mm1_b=mm1_b, h1_b=h1_b, gg_b=gg_b, gqq_b=gqq_b):
                        for ht in range(16):
                            acc = [S0h[0], S0h[1], S1h[0], S1h[1]] if ht % 2 == 0 else [p[:] for p in pb]
                            for tc in range(4):
                                j = 4 * ht + tc
                                scalar.wait_ge(m_mm1.h, mm1_b + j + 1)
                                if j >= 2:
                                    scalar.wait_ge(m_gq.h, gqq_b + j - 1)
                                scalar.copy(h1b[j % 2][:], acc[tc]).then_inc(m_h1.h, 1)
                                scalar.activation(gbv[j % 2][:], h1b[j % 2][:],
                                                  Act.Gelu_apprx_tanh).then_inc(m_g.h, 1)

                    @block.vector
                    def _(vector, half=half, gg_b=gg_b, gqq_b=gqq_b):
                        for j in range(64):
                            vector.wait_ge(m_g.h, gg_b + j + 1)
                            if j >= 2:
                                vector.wait_ge(gqs2[j % 2].h, gqs_hbase[j % 2] + 16 * (j // 2))
                            _quant_ops(vector, gbv[j % 2][:], 16, gamv[j % 2][:], gmsv[j % 2][:],
                                       gslv[j % 2][:], grcv[j % 2][:], gq8v[j % 2][:], gqov[j % 2][:],
                                       v_hz).then_inc(m_gq.h, 1)

                    @block.gpsimd
                    def _(gpsimd, half=half, t0=t0, gqq_b=gqq_b):
                        for j in range(64):
                            ht, tc = divmod(j, 4)
                            gpsimd.wait_ge(m_gq.h, gqq_b + j + 1)
                            gqs2[j % 2].incd(gpsimd.dma_start(
                                out=gq_v[:, t0 + tc, ht * 512:(ht + 1) * 512],
                                in_=gqov[j % 2][:]))

                    @block.sync
                    def _(sync, half=half, gqt_b=gqt_b):
                        for hc in range(64):
                            ht = hc // 4
                            sync.wait_ge(gqs2[0].h, gqs_hbase[0] + 16 * 2 * (ht + 1))
                            sync.wait_ge(gqs2[1].h, gqs_hbase[1] + 16 * 2 * (ht + 1))
                            sync.dma_start_transpose(
                                gqT[:, hc * 512:(hc + 1) * 512],
                                gq_d[half * 512:(half + 1) * 512, hc * 128:(hc + 1) * 128]
                            ).then_inc(m_gqt.h, 16)

                    if PH6_STOP == 1:
                        continue

                    @block.scalar
                    def _(scalar, half=half, w2l_b=w2l_b, mm2_b=mm2_b, mm1_b=mm1_b,
                          fld6=fld6):
                        w_ts.wait(scalar)
                        scalar.wait_ge(m_mm1.h, mm1_b + 64)
                        scalar.wait_ge(s_fld.h, fld6 + 16)
                        for oc in range(16):
                            w2l2[oc % 2].incd(scalar.dma_start(
                                out=wstg[oc % 2][:].rearrange("p (a c) -> p a c", c=128),
                                in_=W2qT_v[:, :, oc * 128:(oc + 1) * 128]))
                            if oc >= 1:
                                scalar.wait_ge(m_mm2.h, mm2_b + oc)
                                scalar.activation(h2T[:, (oc - 1) * 512:oc * 512],
                                                  pb[(oc - 1) % 2][:], Act.Identity,
                                                  bias=b2sb[:, oc - 1:oc]).then_inc(m_h2.h, 1)
                        scalar.wait_ge(m_mm2.h, mm2_b + 16)
                        scalar.activation(h2T[:, 15 * 512:16 * 512], pb[15 % 2][:],
                                          Act.Identity,
                                          bias=b2sb[:, 15:16]).then_inc(m_h2.h, 1)

                    @block.tensor
                    def _(tensor, half=half, w2l_b=w2l_b, mm2_b=mm2_b, h2_b=h2_b,
                          gqt_b=gqt_b, h1_b=h1_b):
                        tensor.wait_ge(m_h1.h, h1_b + 64)
                        tensor.wait_ge(m_gqt.h, gqt_b + 16 * 64)
                        for oc in range(16):
                            tensor.wait_ge(w2l2[oc % 2].h, w2l_hbase[oc % 2] + 16 * (oc // 2 + 1))
                            acc = pb[oc % 2][:]
                            if oc >= 2:
                                tensor.wait_ge(m_h2.h, h2_b + oc - 1)
                            for hc in range(64):
                                mm = tensor.matmul(acc,
                                                   wstg[oc % 2][:, hc * 128:(hc + 1) * 128],
                                                   gqT[:, hc * 512:(hc + 1) * 512],
                                                   start=(hc == 0), stop=(hc == 63))
                            mm.then_inc(m_mm2.h, 1)

                    if PH6_STOP == 2:
                        continue

                    @block.sync
                    def _(sync, half=half, t0=t0, out_b=out_b, rl_b=rl_b, so_b=so_b,
                          so_prev_end=so_prev_end):
                        for tc in range(4):
                            if tc >= 2:
                                sync.wait_ge(m_out.h, out_b + tc - 1)
                                sync.wait_ge(s_out.h, so_b + 16 * (tc - 1))
                            if half == 1 and tc == 0:
                                sync.wait_ge(s_out.h, so_prev_end)
                            sync.dma_start(out=xfrl2[tc % 2][:],
                                           in_=xf_v[:, t0 + tc, :]).then_inc(m_rl.h, 16)
                            sync.dma_start(out=x2rl2[tc % 2][:],
                                           in_=x2_v[:, t0 + tc, :]).then_inc(m_rl.h, 16)

                    @block.tensor
                    def _(tensor, half=half, h2_b=h2_b, tr_b6=tr_b6, trc_b=trc_b):
                        tensor.wait_ge(m_h2.h, h2_b + 16)
                        for tc in range(4):
                            Sb = S0b16 if tc % 2 == 0 else S1b16
                            if tc >= 2:
                                tensor.wait_ge(m_trc.h, trc_b + tc - 1)
                            for oc in range(16):
                                tensor.transpose(
                                    Sb[:, oc * 128:(oc + 1) * 128],
                                    h2T[:, oc * 512 + tc * 128: oc * 512 + tc * 128 + 128],
                                    id16[:]).then_inc(m_tr.h, 1)

                    @block.vector
                    def _(vector, half=half, tr_b6=tr_b6, trc_b=trc_b, rl_b=rl_b, out_b=out_b,
                          so_b=so_b):
                        for tc in range(4):
                            Sb = S0b16 if tc % 2 == 0 else S1b16
                            vector.wait_ge(m_tr.h, tr_b6 + 16 * (tc + 1))
                            vector.wait_ge(m_rl.h, rl_b + 32 * (tc + 1))
                            if tc >= 2:
                                vector.wait_ge(s_out.h, so_b + 16 * tc)
                            vector.tensor_add(xfrl2[tc % 2][:], xfrl2[tc % 2][:],
                                              Sb[:, 0:2048]).then_inc(m_trc.h, 1)
                            vector.tensor_add(x2rl2[tc % 2][:], xfrl2[tc % 2][:],
                                              x2rl2[tc % 2][:]).then_inc(m_out.h, 1)

                    @block.gpsimd
                    def _(gpsimd, half=half, t0=t0, out_b=out_b):
                        for tc in range(4):
                            gpsimd.wait_ge(m_out.h, out_b + tc + 1)
                            gpsimd.dma_start(out=out_v[:, t0 + tc, :],
                                             in_=x2rl2[tc % 2][:]).then_inc(s_out.h, 16)

                if PH6_DBG and PH6_STOP == 0:
                    @block.gpsimd
                    def _(gpsimd):
                        gpsimd.wait_ge(s_out.h, s_out.n)
                        s_out.incd(gpsimd.dma_start(out=dbg["hqT6"], in_=hqT[:]))
                        s_out.incd(gpsimd.dma_start(out=dbg["gqT6"], in_=gqT[:]))
                        s_out.incd(gpsimd.dma_start(out=dbg["h2T6"], in_=h2T[:]))
                        s_out.incd(gpsimd.dma_start(out=dbg["gq6"], in_=gq_d))
                        s_out.incd(gpsimd.dma_start(out=dbg["xfd6"], in_=xf_d))
                        s_out.incd(gpsimd.dma_start(out=dbg["x2d6"], in_=x2_d))

            @block.gpsimd
            def _(gpsimd):
                if PH6_STOP == 0:
                    s_out.wait(gpsimd)
                elif PH6_STOP == 1:
                    gpsimd.wait_ge(gqs2[0].h, gqs2[0].n)
                    gpsimd.wait_ge(gqs2[1].h, gqs2[1].n)
                    gpsimd.wait_ge(m_gqt.h, m_gqt.n)
                    gpsimd.wait_ge(w_ts.h, w_ts.n)
                else:
                    gpsimd.wait_ge(m_h2.h, m_h2.n)

        if STAGE < 5:
            emit_fence()
            dbg_srcs = {"q16": q16, "k16": k16, "v16": v16, "murow": murow,
                        "rstdrow": rstdrow, "negM": negM}
            if STAGE >= 3:
                dbg_srcs["aoT"] = aoT

            @block.gpsimd
            def _(gpsimd):
                for name in ("q16", "k16", "v16", "murow", "rstdrow"):
                    s_out.incd(gpsimd.dma_start(out=dbg[name], in_=dbg_srcs[name][:]))
                if STAGE >= 2:
                    s_out.incd(gpsimd.dma_start(out=dbg["negM"], in_=dbg_srcs["negM"][:]))
                if STAGE >= 3:
                    s_out.incd(gpsimd.dma_start(out=dbg["aoT"], in_=dbg_srcs["aoT"][:]))
                if STAGE >= 4:
                    s_out.incd(gpsimd.dma_start(out=dbg["hq"], in_=hq_d))
                    s_out.incd(gpsimd.dma_start(out=dbg["x2"], in_=x2_d))
                    s_out.incd(gpsimd.dma_start(out=dbg["xf"], in_=xf_d))
                    st45, sa45, sb45 = nonlocal_dbg45
                    s_out.incd(gpsimd.dma_start(out=dbg["stats"], in_=st45[:]))
                    with nc.allow_non_contiguous_dma("debug dump"):
                        s_out.incd(gpsimd.dma_start(out=dbg["sacca"][:, 0:1], in_=sa45[:]))
                        s_out.incd(gpsimd.dma_start(out=dbg["sacca"][:, 1:2], in_=sb45[:]))
                if STAGE >= 3:
                    s_out.incd(gpsimd.dma_start(out=dbg["W1q"], in_=W1q))
                    s_out.incd(gpsimd.dma_start(out=dbg["W2q"], in_=W2q))
                s_out.wait(gpsimd)

    nc.finalize()
    return nc


def _rope_maps():
    pos = 10000.0 ** ((-2.0 * np.arange(0, 128, 2, dtype=np.float64) - 1.0) / 128.0)
    ang = np.arange(T, dtype=np.float64)[:, None] * pos[None, :]
    cosT = np.cos(ang).T
    sinT = np.sin(ang).T
    cosm = np.concatenate([cosT, cosT], axis=0)
    sinm = np.concatenate([-sinT, sinT], axis=0)
    return cosm.astype(np.float32), sinm.astype(np.float32)


def prepare_in_maps(L0_attn_x, L0_attn_w_attn, L0_attn_w_out, L0_ffn_ln_w,
                    L0_ffn_ln_b, L0_ffn_W1, L0_ffn_b1, L0_ffn_W2, L0_ffn_b2):
    x = np.ascontiguousarray(np.asarray(L0_attn_x, dtype=np.float32))
    w_attn = np.ascontiguousarray(np.asarray(L0_attn_w_attn, dtype=np.float32))
    w_out = np.ascontiguousarray(np.asarray(L0_attn_w_out, dtype=np.float32))
    ln_w = np.asarray(L0_ffn_ln_w, dtype=np.float32)
    ln_b = np.asarray(L0_ffn_ln_b, dtype=np.float32)
    W1 = np.ascontiguousarray(np.asarray(L0_ffn_W1, dtype=np.float32))
    b1 = np.asarray(L0_ffn_b1, dtype=np.float32)
    W2 = np.ascontiguousarray(np.asarray(L0_ffn_W2, dtype=np.float32))
    b2 = np.asarray(L0_ffn_b2, dtype=np.float32)

    cosm, sinm = _rope_maps()
    idn = np.eye(128, dtype=ml_dtypes.bfloat16)
    wq_h = np.ascontiguousarray(w_attn[:, 0:2048])
    wkv_h = np.ascontiguousarray(w_attn[:, 2048:2304])
    lnwx = np.ascontiguousarray(np.broadcast_to(ln_w[None, :], (128, D)))
    lnbx = np.ascontiguousarray(np.broadcast_to(ln_b[None, :], (128, D)))
    b2cm = np.ascontiguousarray(b2.reshape(16, 128).T)
    b1r = np.ascontiguousarray(b1[None, :])

    in_maps = []
    for c in range(8):
        b, qr = divmod(c, 4)
        xb = x[b]
        xTr = np.ascontiguousarray(np.roll(xb.T, -qr * NT, axis=1))
        cosr = np.ascontiguousarray(np.roll(cosm, -qr * NT, axis=1))
        sinr = np.ascontiguousarray(np.roll(sinm, -qr * NT, axis=1))
        xo = np.ascontiguousarray(xb[qr * NT:(qr + 1) * NT, :])
        in_maps.append({
            "xT": xTr, "xov": xo, "wqd": wq_h, "wkv": wkv_h, "wod": w_out,
            "W1": W1, "W2": W2, "cosk": cosr, "sink": sinr, "idn": idn,
            "b1dr": b1r, "b2c": b2cm, "lnwx": lnwx, "lnbx": lnbx,
        })
    return in_maps


_NC_CACHE = {}


def kernel(**inputs):
    in_maps = prepare_in_maps(**inputs)
    if STAGE not in _NC_CACHE:
        _NC_CACHE[STAGE] = build_nc()
    nc = _NC_CACHE[STAGE]
    res = run_bass_kernel_spmd(nc, in_maps, core_ids=list(range(8)))
    out = np.empty((2, T, D), dtype=np.float32)
    for c in range(8):
        b, qr = divmod(c, 4)
        out[b, qr * NT:(qr + 1) * NT, :] = res.results[c]["out"]
    return out
